# version 16
# baseline (speedup 1.0000x reference)
import numpy as np
import ml_dtypes
import concourse.bass as bass
import concourse.mybir as mybir
from concourse.bass_utils import run_bass_kernel_spmd

F32 = mybir.dt.float32
BF16 = mybir.dt.bfloat16
AF = mybir.ActivationFunctionType
ALU = mybir.AluOpType
AX = mybir.AxisListType

NCORES = 8
NB = 2
S = 2048
D = 1024
NT = S // 128
DC = D // 128
NE = 8
ISZ = 1408
OC2 = 11
IC = ISZ // 128
TCH = 1024
NTC = S // TCH
TAU = TCH // 128
NEG = -1e30

_BF = ml_dtypes.bfloat16


class Tok:
    __slots__ = ("kind", "eng", "ord")
    def __init__(self, kind, eng, o):
        self.kind, self.eng, self.ord = kind, eng, o


class Buf:
    __slots__ = ("name", "w", "r", "psum")
    def __init__(self, name, psum=False):
        self.name, self.w, self.r, self.psum = name, None, [], psum


class Prog:
    ENGS = ("tensor", "vector", "scalar", "gpsimd", "sync")
    DMAQ = ("gpsimd", "sync", "scalar")

    def __init__(self, nc, stack):
        self.nc = nc
        self.stack = stack
        self.recs = {e: [] for e in self.ENGS}
        self.n = {e: 0 for e in self.ENGS}
        self.SE = {e: stack.enter_context(nc.semaphore(f"se_{e}"))
                   for e in self.ENGS}
        self.QS = {}
        self.wm = {e: {} for e in self.ENGS}

    RROT = 6

    def _qs(self, q, klass):
        k = (q, klass)
        if k not in self.QS:
            self.QS[k] = [[self.stack.enter_context(
                self.nc.semaphore(f"q_{q}_{klass}{j}"))
                for j in range(self.RROT)], 0]
        return self.QS[k]

    def _deps_from(self, eng, reads, writes):
        deps = []
        for b in reads:
            if b.w is not None:
                deps.append(b.w)
            if b.psum:
                deps.extend(t for t in b.r if t.eng != eng)
        for b in writes:
            if b.w is not None:
                deps.append(b.w)
            deps.extend(b.r)
        return deps

    def _waits(self, eng, deps):
        best = {}
        for t in deps:
            if t.kind == "e":
                if t.eng == eng and eng == "tensor":
                    continue
                sem, val = self.SE[t.eng], t.ord + 1
            else:
                sems = self.QS[t.eng][0]
                sem = sems[t.ord % self.RROT]
                val = 16 * (t.ord // self.RROT + 1)
            k = id(sem)
            if val > best.get(k, (None, -1))[1]:
                best[k] = (sem, val)
        out = []
        wm = self.wm[eng]
        for k, (sem, val) in best.items():
            if val > wm.get(k, 0):
                out.append((sem, val))
                wm[k] = val
        return out

    def _upd(self, tok, reads, writes):
        for b in reads:
            b.r.append(tok)
        for b in writes:
            b.w = tok
            b.r = []

    def op(self, eng, fns, reads=(), writes=(), tag=""):
        if not isinstance(fns, (list, tuple)):
            fns = [fns]
        deps = self._deps_from(eng, reads, writes)
        waits = self._waits(eng, deps)
        tok = Tok("e", eng, self.n[eng])
        self.n[eng] += 1
        self.recs[eng].append((waits, list(fns), (self.SE[eng], 1), tag))
        self._upd(tok, reads, writes)
        return tok

    def dma(self, q, fn, reads=(), writes=(), tag="", transpose=False,
            klass="d"):
        if transpose:
            klass = "t"
        deps = self._deps_from(q, reads, writes)
        waits = self._waits(q, deps)
        qs = self._qs(q, klass)
        tok = Tok("d", (q, klass), qs[1])
        sem = qs[0][qs[1] % self.RROT]
        qs[1] += 1
        self.recs[q].append((waits, [fn], (sem, 16), tag))
        self._upd(tok, reads, writes)
        return tok

    def raw_wait(self, eng, sem, val):
        self.recs[eng].append(([(sem, val)], [], None, "final"))

    def dma_barrier(self, q, bufs, klass="d"):
        qs = self._qs(q, klass)
        waits = []
        for j in range(self.RROT):
            nj = (qs[1] - j + self.RROT - 1) // self.RROT
            if nj > 0:
                waits.append((qs[0][j], 16 * nj))
                self.wm[q][id(qs[0][j])] = max(
                    self.wm[q].get(id(qs[0][j]), 0), 16 * nj)
        tok = Tok("e", q, self.n[q])
        self.n[q] += 1
        self.recs[q].append((waits, [lambda e: e.nop()],
                             (self.SE[q], 1), "dmabar"))
        for b in bufs:
            b.w, b.r = tok, []
        return tok

    def rebind(self, old_bufs, new_bufs):
        toks = []
        for b in old_bufs:
            if b.w is not None:
                toks.append(b.w)
            toks.extend(b.r)
            b.w, b.r = None, []
        for b in new_bufs:
            b.r = b.r + toks

    def emit(self, block):
        def body_for(eng_name):
            recs = self.recs[eng_name]
            def body(e):
                for waits, fns, inc, _tag in recs:
                    for sem, val in waits:
                        e.wait_ge(sem, val)
                    for i, fn in enumerate(fns):
                        ins = fn(e)
                        if inc is not None and i == len(fns) - 1:
                            ins.then_inc(inc[0], inc[1])
            return body
        block.tensor(body_for("tensor"))
        block.vector(body_for("vector"))
        block.scalar(body_for("scalar"))
        block.gpsimd(body_for("gpsimd"))
        block.sync(body_for("sync"))


def build(stage="full", opts=frozenset()):
    from contextlib import ExitStack
    nc = bass.Bass("TRN2", target_bir_lowering=False)
    st = ExitStack()

    x_in = nc.dram_tensor("x", [NB, S, D], F32, kind="ExternalInput").ap()
    wq_in = nc.dram_tensor("wq", [D, D], F32, kind="ExternalInput").ap()
    wk_in = nc.dram_tensor("wk", [D, D], F32, kind="ExternalInput").ap()
    wv_in = nc.dram_tensor("wv", [D, D], F32, kind="ExternalInput").ap()
    gw_in = nc.dram_tensor("gw", [NE, D], F32, kind="ExternalInput").ap()
    win_in = nc.dram_tensor("win", [NE, 2 * ISZ, D], F32, kind="ExternalInput").ap()
    wout_in = nc.dram_tensor("wout", [NE, D, ISZ], F32, kind="ExternalInput").ap()
    id_in = nc.dram_tensor("ident", [128, 128], BF16, kind="ExternalInput").ap()
    mk_in = nc.dram_tensor("masks", [4, 128, 512], BF16, kind="ExternalInput").ap()
    x_out = nc.dram_tensor("x_out", [NB, S, D], F32, kind="ExternalOutput").ap()

    s_win0 = nc.dram_tensor("s_win0", [NE, 2 * ISZ, D], BF16, kind="Internal").ap()
    s_winT = nc.dram_tensor("s_winT", [NE, DC, 128, 2 * ISZ], BF16, kind="Internal").ap()
    s_wout = nc.dram_tensor("s_wout", [NE, IC, D, 128], BF16, kind="Internal").ap()
    s_x = nc.dram_tensor("s_x", [NB, DC, S, 128], BF16, kind="Internal").ap()

    sb_top = nc._sbuf_addr_for_side("right")
    sb_base = 17216
    assert sb_base >= nc._sbuf_addr_for_side("left") + 700, (
        sb_base, nc._sbuf_addr_for_side("left"))
    off = [sb_base]
    def alloc(name, cols, dt, parts=128, at=None):
        nbytes = cols * (2 if dt == BF16 else 4)
        if at is None:
            at = (off[0] + 31) & ~31
            off[0] = at + nbytes
        t = nc.alloc_sbuf_tensor_at(name, [parts, cols], dt, offset=at)
        return t

    id_sb = alloc("id_sb", 128, BF16)
    mask_sb = alloc("mask_sb", 4 * 512, BF16)
    gwT_sb = alloc("gwT_sb", DC * 8, BF16)
    gw_bf = alloc("gw_bf", D, BF16, parts=8)
    stats = alloc("stats", 64, F32)
    wg_sb = alloc("wg_sb", 64, F32)
    gt_sb = alloc("gt_sb", 64, F32)

    base = (off[0] + 63) & ~63
    R0 = base
    R1 = R0 + 32768
    R2 = R1 + 32768
    R3 = R2 + 32768
    R4 = R3 + 32768
    R5 = R4 + 32768
    END = R5 + 16384
    x_tm = alloc("x_tm", 16 * 1024, BF16, at=R0)
    XT = alloc("XT", 8 * 2048, BF16, at=R1)
    QT = alloc("QT", 8 * 2048, BF16, at=R2)
    acc = alloc("acc", 8 * 1024, F32, at=R2)
    stg_i = alloc("stg_i", 4096, BF16, at=END)
    stg_o = alloc("stg_o", 4096, BF16, at=END + 8192)
    KT = alloc("KT", 8 * 2048, BF16, at=R3)
    u_buf = alloc("u_buf", 11 * 1024, BF16, at=R3)
    sv_sb = [alloc(f"sv{k}", 512, BF16, at=R3 + 22528 + 1024 * k) for k in range(2)]
    V_tm = alloc("V_tm", 16 * 1024, BF16, at=R4)
    winT = [alloc(f"winT{k}", 8 * 256, BF16, at=R4 + 4096 * k) for k in range(2)]
    woutT = alloc("woutT", 11 * 1024, BF16, at=R4 + 8192)
    Wq_sb = alloc("Wq_sb", 8 * 1024, BF16, at=R5)
    P_sb = alloc("P_sb", 2 * 2048, BF16, at=R5)
    PT_sb = alloc("PT_sb", 16 * 128, BF16, at=R5 + 8192)
    tmp_sb2 = [alloc(f"tmp_sb{k}", 1024, F32, at=END + 16384 + 4096 * k)
               for k in range(2)]
    assert END + 24576 <= sb_top, (END, sb_top)

    psS = nc.alloc_psum_tensor("psS", [128, 2048], F32)
    psP4 = nc.alloc_psum_tensor("psP4", [128, 512], F32)
    psP5 = nc.alloc_psum_tensor("psP5", [128, 512], F32)
    psC = nc.alloc_psum_tensor("psC", [128, 1024], F32)

    P = Prog(nc, st)

    PS = [Buf(f"ps{i}", psum=True) for i in range(8)]
    def ps_bufs(lo_bank, hi_bank):
        return [PS[i] for i in range(lo_bank, hi_bank + 1)]
    B_id = Buf("id"); B_mask = Buf("mask"); B_gwT = Buf("gwT"); B_gwbf = Buf("gwbf")
    B_stats = [Buf(f"st{i}") for i in range(24)]
    B_wg = Buf("wg"); B_gt = Buf("gt")
    B_stgi = [Buf("stgi")]
    B_stgo = [Buf("stgo")]
    B_stg4 = [Buf(f"stg4_{k}") for k in range(4)]
    B_xtm = [Buf(f"xtm{t}") for t in range(16)]
    B_XT = [Buf(f"XT{c}") for c in range(8)]
    B_X1T = [Buf(f"X1T{c}") for c in range(8)]
    B_QT = [Buf(f"QT{c}") for c in range(8)]
    B_acc = [Buf(f"acc{t}") for t in range(TAU)]
    B_KT = [Buf(f"KT{c}") for c in range(8)]
    B_u = [Buf(f"u{o}") for o in range(22)]
    B_sv = [Buf(f"sv{k}") for k in range(2)]
    B_V = [Buf(f"V{t}") for t in range(16)]
    B_winT = [Buf(f"winT{k}") for k in range(2)]
    B_woutT = [Buf(f"woutT{i}") for i in range(IC)]
    B_wq = Buf("wq")
    B_P = [Buf("P0"), Buf("P1")]
    B_PT = [Buf(f"PT{c}") for c in range(16)]
    B_tmp = [Buf("tmp0"), Buf("tmp1")]
    B_swin0 = [Buf(f"sw0_{e}") for e in range(NE)]
    B_swinT = [Buf(f"swT{e}") for e in range(NE)]
    B_swout = [[Buf(f"swo{e}_{i}") for i in range(IC)] for e in range(NE)]
    B_sx = [[Buf(f"sx{b}_{c}") for c in range(DC)] for b in range(NB)]

    def stat(c):
        return stats[:, c:c + 1]
    SM_NMAX = 0
    SM_RSUM = 2
    SM_RINV = 4
    SM_L1S = 6
    SM_NMEAN = 8
    SM_VAR = 10
    SM_RSTD = 12
    SM_EPS = 14
    SM_M1 = 16
    SM_M2 = 18
    SM_P1 = 20
    SM_P2 = 22

    P.dma("sync", lambda e: e.dma_start(out=id_sb[:, :], in_=id_in), writes=[B_id])
    P.dma("sync", lambda e: e.dma_start(
        out=mask_sb[:, :].rearrange("p (r k) -> p r k", k=512),
        in_=mk_in.rearrange("r p k -> p r k")), writes=[B_mask])
    P.op("vector", lambda e: e.memset(stats[:, SM_EPS:SM_EPS + 2], 1e-5),
         writes=[B_stats[SM_EPS]])

    def emit_sx(b, extra_fns=(), extra_bufs=()):
        for dc in range(DC):
            for h in range(2):
                def _xs(b=b, dc=dc, h=h):
                    return lambda g: g.dma_start(
                        out=s_x[b, dc, h * 1024:(h + 1) * 1024, :],
                        in_=x_in[b, h * 1024:(h + 1) * 1024,
                                 dc * 128:(dc + 1) * 128])
                P.dma("gpsimd", _xs(), writes=[B_sx[b][dc]], tag="x_s", klass="cast")
        for fn, bufs in zip(extra_fns, extra_bufs):
            P.dma("gpsimd", fn, writes=bufs, tag="x_s_extra", klass="cast")
        P.dma_barrier("gpsimd", [B_sx[b][dc] for dc in range(DC)]
                      + [bb for bufs in extra_bufs for bb in bufs],
                      klass="cast")

    emit_sx(0, extra_fns=[lambda e: e.dma_start(out=gw_bf[:, :], in_=gw_in)],
            extra_bufs=[[B_gwbf]])

    def _gw_tp(dc):
        return lambda e: e.matmul(psP4[:, dc * 8:(dc + 1) * 8],
                                  gw_bf[0:8, dc * 128:(dc + 1) * 128],
                                  id_sb[0:8, 0:8], start=True, stop=True)
    P.op("tensor", [_gw_tp(dc) for dc in range(DC)],
         reads=[B_gwbf, B_id], writes=[PS[4]], tag="gwT")
    P.op("vector", lambda e: e.tensor_copy(gwT_sb[:, :], psP4[:, 0:64]),
         reads=[PS[4]], writes=[B_gwT], tag="gwT_ev")

    def emit_casts():
        for e in range(NE):
            for q in range(4):
                def _ca(e=e, q=q):
                    return lambda g: g.dma_start(
                        out=s_win0[e, q * 704:(q + 1) * 704, :],
                        in_=win_in[e, q * 704:(q + 1) * 704, :])
                P.dma("gpsimd", _ca(), writes=[B_swin0[e]], tag="wi_cast", klass="cast")
            P.dma_barrier("gpsimd", [B_swin0[e]], klass="cast")
        for e in range(NE):
            for i in range(IC):
                def _wo(e=e, i=i):
                    return lambda g: g.dma_start(
                        out=s_wout[e, i],
                        in_=wout_in[e, :, i * 128:(i + 1) * 128])
                P.dma("gpsimd", _wo(), writes=[B_swout[e][i]], tag="wo_cast", klass="cast")
        P.dma_barrier("gpsimd", [b2 for row in B_swout for b2 in row], klass="cast")

    def chain_chunk(e, c):
        sl = c % 2
        inb = stg_i[:, 0:2048] if sl == 0 else stg_i[:, 2048:4096]
        outb = stg_o[:, 0:2048] if sl == 0 else stg_o[:, 2048:4096]
        binb = B_stg4[sl]
        boutb = B_stg4[2 + sl]
        def _ld(e=e, c=c, inb=inb):
            return lambda s: s.dma_start(
                out=inb.rearrange("p (s d) -> p s d", d=1024),
                in_=s_win0[e, c * 256:(c + 1) * 256, :]
                    .rearrange("(s p) d -> p s d", p=128))
        P.dma("sync", _ld(), reads=[B_swin0[e]], writes=[binb], tag="wi_ld",
              klass="cl")
        for r in range(4):
            bank = psP4 if r % 2 == 0 else psP5
            hbuf = PS[4 + (r % 2)]
            fns = []
            for j in range(4):
                k = r * 4 + j
                s2, dc = k // 8, k % 8
                def _tp(s2=s2, dc=dc, j=j, bank=bank, inb=inb):
                    return lambda t: t.matmul(
                        bank[:, j * 128:(j + 1) * 128],
                        inb[:, s2 * 1024 + dc * 128: s2 * 1024 + (dc + 1) * 128],
                        id_sb[:, :], start=True, stop=True)
                fns.append(_tp())
            P.op("tensor", fns, reads=[binb, B_id], writes=[hbuf], tag="wi_tp")
            s2 = (r * 4) // 8
            dc0 = (r * 4) % 8
            eng = "vector" if r % 2 == 0 else "scalar"
            def _ev(s2=s2, dc0=dc0, bank=bank, outb=outb, eng=eng):
                def f(v):
                    dst = outb.rearrange("p (dc o) -> p dc o", o=256)[
                        :, dc0:dc0 + 4, s2 * 128:(s2 + 1) * 128]
                    src = bank[:, :].rearrange("p (j o) -> p j o", o=128)
                    if eng == "vector":
                        return v.tensor_copy(dst, src)
                    return v.copy(dst, src)
                return f
            P.op(eng, _ev(), reads=[hbuf], writes=[boutb], tag="wi_ev")
        def _st(e=e, c=c, outb=outb):
            return lambda s: s.dma_start(
                out=s_winT[e, :, :, c * 256:(c + 1) * 256]
                    .rearrange("dc p o -> p dc o"),
                in_=outb.rearrange("p (dc o) -> p dc o", o=256))
        P.dma("sync", _st(), reads=[boutb], writes=[B_swinT[e]],
              tag="wi_st", klass="st")


    xt_pre = [False]
    wq_pre = [False]
    wout_resident = [-1]
    for b in range(NB):
        if b > 0:
            if not xt_pre[0]:
                P.rebind(B_X1T, B_XT)
            P.rebind(B_acc, B_QT)
            P.rebind(B_u + B_sv, B_KT)
            P.rebind(B_winT + B_woutT, B_V)
            if not wq_pre[0]:
                P.rebind(B_P + B_PT, [B_wq])

        def emit_xtm(b2, h):
            def _xl(b2=b2, h=h):
                return lambda g: g.dma_start(
                    out=x_tm[:, :].rearrange("p (t d) -> p t d", d=D)
                    [:, h * 8:(h + 1) * 8, :],
                    in_=x_in[b2, h * 1024:(h + 1) * 1024, :]
                        .rearrange("(t p) d -> p t d", p=128))
            P.dma("gpsimd", _xl(), writes=B_xtm[h * 8:(h + 1) * 8], tag="x_tm", klass="ld")
        if b == 0:
            for h in range(2):
                emit_xtm(0, h)
        elif stage == "attn":
            emit_sx(b)
            for h in range(2):
                emit_xtm(b, h)
        if not xt_pre[0]:
            for dc in range(DC):
                def _xt(b=b, dc=dc):
                    return lambda s: s.dma_start_transpose(
                        XT[:, dc * 2048:(dc + 1) * 2048], s_x[b, dc])
                P.dma("sync", _xt(), reads=[B_sx[b][dc]], writes=[B_XT[dc]],
                      tag="XT", transpose=True)
        xt_pre[0] = False

        for wi, (w_ap, dst, dstbufs, kind) in enumerate([
                (wq_in, QT, B_QT, "q"), (wk_in, KT, B_KT, "k"),
                (wv_in, V_tm, B_V, "v")]):
            if kind == "q" and wq_pre[0]:
                wq_pre[0] = False
            else:
                def _wl(w_ap=w_ap):
                    return lambda g: g.dma_start(
                        out=Wq_sb[:, :].rearrange("p (dc o) -> p dc o", o=D),
                        in_=w_ap.rearrange("(dc p) o -> p dc o", p=128))
                P.dma("gpsimd", _wl(), writes=[B_wq], tag=f"w{kind}_l", klass="ld")
            if kind in ("q", "k"):
                for mo in range(8):
                    for tg in range(4):
                        bank = psP4 if (mo * 4 + tg) % 2 == 0 else psP5
                        bbuf = PS[4] if (mo * 4 + tg) % 2 == 0 else PS[5]
                        fns = []
                        for dc in range(DC):
                            def _mm(dc=dc, mo=mo, tg=tg, bank=bank):
                                return lambda t: t.matmul(
                                    bank[:, :],
                                    Wq_sb[:, dc * D + mo * 128: dc * D + (mo + 1) * 128],
                                    XT[:, dc * 2048 + tg * 512: dc * 2048 + (tg + 1) * 512],
                                    start=(dc == 0), stop=(dc == DC - 1))
                            fns.append(_mm())
                        P.op("tensor", fns, reads=[B_wq] + B_XT, writes=[bbuf],
                             tag=f"{kind}gen")
                        def _ev(mo=mo, tg=tg, bank=bank, dst=dst):
                            return lambda v: v.tensor_copy(
                                dst[:, mo * 2048 + tg * 512: mo * 2048 + (tg + 1) * 512],
                                bank[:, :])
                        P.op("vector", _ev(), reads=[bbuf], writes=[dstbufs[mo]],
                             tag=f"{kind}ev")
            else:
                for tt in range(16):
                    for eh in range(2):
                        bank = psP4 if (tt * 2 + eh) % 2 == 0 else psP5
                        bbuf = PS[4] if (tt * 2 + eh) % 2 == 0 else PS[5]
                        fns = []
                        for dc in range(DC):
                            def _mm(dc=dc, tt=tt, eh=eh, bank=bank):
                                return lambda t: t.matmul(
                                    bank[:, :],
                                    XT[:, dc * 2048 + tt * 128: dc * 2048 + (tt + 1) * 128],
                                    Wq_sb[:, dc * D + eh * 512: dc * D + (eh + 1) * 512],
                                    start=(dc == 0), stop=(dc == DC - 1))
                            fns.append(_mm())
                        P.op("tensor", fns, reads=[B_wq] + B_XT, writes=[bbuf],
                             tag="vgen")
                        def _ev(tt=tt, eh=eh, bank=bank):
                            return lambda v: v.tensor_copy(
                                V_tm[:, tt * 1024 + eh * 512: tt * 1024 + (eh + 1) * 512],
                                bank[:, :])
                        P.op("vector", _ev(), reads=[bbuf], writes=[B_V[tt]],
                             tag="vev")

        P.rebind(B_XT, B_X1T)
        P.rebind([B_wq], B_P + B_PT)

        def emit_scores(t):
            jd = t // 4
            r = t % 4
            W = 512 * (jd + 1)
            par = t % 2
            for kg in range(jd + 1):
                fns = []
                for dc in range(DC):
                    def _mm(dc=dc, kg=kg, t=t):
                        return lambda te: te.matmul(
                            psS[:, kg * 512:(kg + 1) * 512],
                            QT[:, dc * 2048 + t * 128: dc * 2048 + (t + 1) * 128],
                            KT[:, dc * 2048 + kg * 512: dc * 2048 + (kg + 1) * 512],
                            start=(dc == 0), stop=False)
                    fns.append(_mm())
                if kg == jd:
                    def _msk(kg=kg, r=r):
                        return lambda te: te.matmul(
                            psS[:, kg * 512:(kg + 1) * 512], id_sb[:, :],
                            mask_sb[:, r * 512:(r + 1) * 512],
                            start=False, stop=True)
                    fns.append(_msk())
                else:
                    def _mm_last(dc=DC - 1, kg=kg, t=t):
                        return lambda te: te.matmul(
                            psS[:, kg * 512:(kg + 1) * 512],
                            QT[:, dc * 2048 + t * 128: dc * 2048 + (t + 1) * 128],
                            KT[:, dc * 2048 + kg * 512: dc * 2048 + (kg + 1) * 512],
                            start=False, stop=True)
                    fns[-1] = _mm_last()
                P.op("tensor", fns, reads=B_QT + B_KT + [B_id, B_mask],
                     writes=[PS[kg]], tag=f"sc{t}_{kg}")
            def _nm(t=t, W=W, par=par):
                return lambda v: v.reduce_max(stat(SM_NMAX + par), psS[:, 0:W],
                                              axis=AX.X, negate=True)
            P.op("vector", _nm(), reads=ps_bufs(0, jd),
                 writes=[B_stats[SM_NMAX + par]], tag=f"nmax{t}")
            def _nms(par=par):
                return lambda v: v.tensor_scalar(
                    stat(SM_NMAX + par), stat(SM_NMAX + par), 1.0 / 32.0, None,
                    op0=ALU.mult)
            P.op("vector", _nms(), reads=[B_stats[SM_NMAX + par]],
                 writes=[B_stats[SM_NMAX + par]], tag=f"nmsc{t}")
            def _exp(t=t, W=W, par=par):
                return lambda s: s.activation(
                    P_sb[:, par * 2048: par * 2048 + W], psS[:, 0:W], AF.Exp,
                    bias=stat(SM_NMAX + par), scale=1.0 / 32.0,
                    accum_out=stat(SM_RSUM + par))
            P.op("scalar", _exp(), reads=ps_bufs(0, jd) + [B_stats[SM_NMAX + par]],
                 writes=[B_P[par], B_stats[SM_RSUM + par]], tag=f"exp{t}")

        def emit_consume(t):
            par = t % 2
            def _rinv(par=par):
                return lambda v: v.reciprocal(stat(SM_RINV + par),
                                              stat(SM_RSUM + par))
            P.op("vector", _rinv(), reads=[B_stats[SM_RSUM + par]],
                 writes=[B_stats[SM_RINV + par]], tag=f"rinv{t}")
            nch = t + 1
            for rnd in range((nch + 3) // 4):
                c0 = rnd * 4
                n = min(4, nch - c0)
                bank = psP4 if rnd % 2 == 0 else psP5
                bbuf = PS[4] if rnd % 2 == 0 else PS[5]
                fns = []
                for j in range(n):
                    def _tp(j=j, c=c0 + j, bank=bank, par=par):
                        return lambda te: te.matmul(
                            bank[:, j * 128:(j + 1) * 128],
                            P_sb[:, par * 2048 + c * 128: par * 2048 + (c + 1) * 128],
                            id_sb[:, :], start=True, stop=True)
                    fns.append(_tp())
                P.op("tensor", fns, reads=[B_P[par], B_id], writes=[bbuf],
                     tag=f"pt{t}_{rnd}")
                def _ev(c0=c0, n=n, bank=bank):
                    return lambda v: v.tensor_copy(
                        PT_sb[:, c0 * 128:(c0 + n) * 128], bank[:, 0:n * 128])
                P.op("vector", _ev(), reads=[bbuf],
                     writes=B_PT[c0:c0 + n], tag=f"ptev{t}_{rnd}")
            fns = []
            for c in range(nch):
                for half in range(2):
                    def _mm(c=c, half=half):
                        return lambda te: te.matmul(
                            psC[:, half * 512:(half + 1) * 512],
                            PT_sb[:, c * 128:(c + 1) * 128],
                            V_tm[:, c * 1024 + half * 512: c * 1024 + (half + 1) * 512],
                            start=(c == 0), stop=(c == nch - 1))
                    fns.append(_mm())
            P.op("tensor", fns, reads=B_PT[0:nch] + B_V[0:nch],
                 writes=[PS[6], PS[7]], tag=f"ctx{t}")
            def _stt(t=t, par=par):
                return lambda v: v.scalar_tensor_tensor(
                    tmp_sb2[par][:, :], psC[:, :], stat(SM_RINV + par),
                    x_tm[:, t * 1024:(t + 1) * 1024],
                    op0=ALU.mult, op1=ALU.add, accum_out=stat(SM_L1S + par))
            P.op("vector", _stt(), reads=[PS[6], PS[7], B_stats[SM_RINV + par],
                                          B_xtm[t]],
                 writes=[B_tmp[par], B_stats[SM_L1S + par]], tag=f"stt{t}")
            def _nmean(par=par):
                return lambda v: v.tensor_scalar(
                    stat(SM_NMEAN + par), stat(SM_L1S + par), -1.0 / D, None,
                    op0=ALU.mult)
            P.op("vector", _nmean(), reads=[B_stats[SM_L1S + par]],
                 writes=[B_stats[SM_NMEAN + par]], tag=f"nmean{t}")
            def _sq(par=par):
                return lambda s: s.activation(
                    psC[:, :], tmp_sb2[par][:, :], AF.Square,
                    bias=stat(SM_NMEAN + par), accum_out=stat(SM_VAR + par))
            P.op("scalar", _sq(), reads=[B_tmp[par], B_stats[SM_NMEAN + par]],
                 writes=[B_stats[SM_VAR + par], PS[6], PS[7]], tag=f"sq{t}")
            def _rln(par=par):
                return lambda s: s.activation(
                    stat(SM_RSTD + par), stat(SM_VAR + par), AF.Ln,
                    bias=stat(SM_EPS), scale=1.0 / D)
            P.op("scalar", _rln(), reads=[B_stats[SM_VAR + par], B_stats[SM_EPS]],
                 writes=[B_stats[SM_RSTD + par]], tag=f"rln{t}")
            def _rex(par=par):
                return lambda s: s.activation(
                    stat(SM_RSTD + par), stat(SM_RSTD + par), AF.Exp, scale=-0.5)
            P.op("scalar", _rex(), reads=[B_stats[SM_RSTD + par]],
                 writes=[B_stats[SM_RSTD + par]], tag=f"rex{t}")
            def _x1(t=t, par=par):
                return lambda v: v.tensor_scalar(
                    x_tm[:, t * 1024:(t + 1) * 1024], tmp_sb2[par][:, :],
                    stat(SM_NMEAN + par), stat(SM_RSTD + par),
                    op0=ALU.add, op1=ALU.mult)
            P.op("vector", _x1(), reads=[B_tmp[par], B_stats[SM_NMEAN + par],
                                         B_stats[SM_RSTD + par]],
                 writes=[B_xtm[t]], tag=f"x1{t}")

        def emit_x1t(t):
            for rnd in range(2):
                bank = psP5 if rnd % 2 == 0 else psP4
                bbuf = PS[5] if rnd % 2 == 0 else PS[4]
                fns = []
                for j in range(4):
                    dc = rnd * 4 + j
                    def _tp(dc=dc, j=j, t=t, bank=bank):
                        return lambda te: te.matmul(
                            bank[:, j * 128:(j + 1) * 128],
                            x_tm[:, t * 1024 + dc * 128: t * 1024 + (dc + 1) * 128],
                            id_sb[:, :], start=True, stop=True)
                    fns.append(_tp())
                P.op("tensor", fns, reads=[B_xtm[t], B_id], writes=[bbuf],
                     tag=f"x1t{t}_{rnd}")
                def _ev(rnd=rnd, t=t, bank=bank):
                    def f(v):
                        dst = XT[:, :].rearrange("p (dc t2) -> p dc t2", t2=2048)[
                            :, rnd * 4:(rnd + 1) * 4, t * 128:(t + 1) * 128]
                        src = bank[:, :].rearrange("p (j o) -> p j o", o=128)
                        return v.tensor_copy(dst, src)
                    return f
                P.op("vector", _ev(), reads=[bbuf],
                     writes=[B_X1T[rnd * 4 + j2] for j2 in range(4)],
                     tag=f"x1tev{t}_{rnd}")

        chain_jobs = []
        if b == 0:
            emit_casts()
            chain_jobs = [(e2, c2) for e2 in range(NE) for c2 in range(OC2)]
        for t in range(NT + 2):
            if t < NT:
                emit_scores(t)
            if t >= 1 and t - 1 < NT:
                emit_consume(t - 1)
            if t >= 2:
                emit_x1t(t - 2)
            if t >= 5:
                for _ in range(11):
                    if chain_jobs:
                        chain_chunk(*chain_jobs.pop(0))

        while chain_jobs:
            chain_chunk(*chain_jobs.pop(0))

        if stage == "attn":
            for h in range(2):
                def _o(b=b, h=h):
                    return lambda g: g.dma_start(
                        out=x_out[b, h * 1024:(h + 1) * 1024, :]
                            .rearrange("(t p) d -> p t d", p=128),
                        in_=x_tm[:, :].rearrange("p (t d) -> p t d", d=D)
                        [:, h * 8:(h + 1) * 8, :])
                P.dma("gpsimd", _o(), reads=B_xtm[h * 8:(h + 1) * 8], tag="dbg_out", klass="out")
            continue

        wout_resident[0] = -1
        P.rebind(B_QT, B_acc)
        P.rebind(B_KT, B_u + B_sv)
        P.rebind(B_V, B_winT + B_woutT)

        for tc in range(NTC):
            tb = tc * TAU
            for tau in range(TAU):
                fns = []
                for dc in range(DC):
                    def _mm(dc=dc, tau=tau, tb=tb):
                        return lambda te: te.matmul(
                            psC[:, tau * 8:(tau + 1) * 8],
                            XT[:, dc * 2048 + (tb + tau) * 128:
                               dc * 2048 + (tb + tau + 1) * 128],
                            gwT_sb[:, dc * 8:(dc + 1) * 8],
                            start=(dc == 0), stop=(dc == DC - 1))
                    fns.append(_mm())
                P.op("tensor", fns, reads=B_X1T + [B_gwT], writes=[PS[6]],
                     tag=f"gate{tc}_{tau}")
            P.op("vector", lambda e: e.tensor_copy(wg_sb[:, :], psC[:, 0:64]),
                 reads=[PS[6]], writes=[B_wg], tag=f"gatev{tc}")
            def emit_gating(tau):
                g8 = wg_sb[:, tau * 8:(tau + 1) * 8]
                P.op("vector", (lambda g8=g8: (lambda v: v.reduce_max(
                    stat(SM_M1), g8, axis=AX.X)))(),
                    reads=[B_wg], writes=[B_stats[SM_M1]], tag="g_m1")
                P.op("vector", (lambda g8=g8: (lambda v: v.tensor_scalar(
                    gt_sb[:, 0:8], g8, stat(SM_M1), None, op0=ALU.is_equal)))(),
                    reads=[B_wg, B_stats[SM_M1]], writes=[B_gt], tag="g_eq1")
                P.op("vector", (lambda g8=g8: (lambda v: v.scalar_tensor_tensor(
                    gt_sb[:, 8:16], gt_sb[:, 0:8], NEG, g8,
                    op0=ALU.mult, op1=ALU.add)))(),
                    reads=[B_gt, B_wg], writes=[B_gt], tag="g_gm")
                P.op("vector", lambda v: v.reduce_max(stat(SM_M2), gt_sb[:, 8:16],
                                                      axis=AX.X),
                     reads=[B_gt], writes=[B_stats[SM_M2]], tag="g_m2")
                P.op("vector", lambda v: v.tensor_scalar(
                    stat(SM_M2 + 1), stat(SM_M1), stat(SM_M2), None,
                    op0=ALU.subtract),
                    reads=[B_stats[SM_M2], B_stats[SM_M1]],
                    writes=[B_stats[SM_M2 + 1]], tag="g_d")
                P.op("scalar", lambda s: s.activation(
                    stat(SM_P2), stat(SM_M2 + 1), AF.Exp),
                    reads=[B_stats[SM_M2 + 1]], writes=[B_stats[SM_P2]], tag="g_q")
                P.op("vector", lambda v: v.tensor_scalar(
                    stat(SM_P2), stat(SM_P2), 1.0, None, op0=ALU.add),
                    reads=[B_stats[SM_P2]], writes=[B_stats[SM_P2]], tag="g_1q")
                P.op("vector", lambda v: v.reciprocal(stat(SM_P2), stat(SM_P2)),
                     reads=[B_stats[SM_P2]], writes=[B_stats[SM_P2]], tag="g_p2")
                P.op("vector", lambda v: v.tensor_scalar(
                    stat(SM_P1), stat(SM_P2), -1.0, 1.0, op0=ALU.mult, op1=ALU.add),
                    reads=[B_stats[SM_P2]], writes=[B_stats[SM_P1]], tag="g_p1")
                P.op("vector", lambda v: v.tensor_scalar(
                    gt_sb[:, 16:24], gt_sb[:, 8:16], stat(SM_M2), None,
                    op0=ALU.is_equal),
                    reads=[B_gt, B_stats[SM_M2]], writes=[B_gt], tag="g_eq2")
                P.op("vector", lambda v: v.tensor_scalar(
                    gt_sb[:, 24:32], gt_sb[:, 0:8], stat(SM_P1), None,
                    op0=ALU.mult),
                    reads=[B_gt, B_stats[SM_P1]], writes=[B_gt], tag="g_t1")
                P.op("vector", (lambda g8=g8: (lambda v: v.scalar_tensor_tensor(
                    g8, gt_sb[:, 16:24], stat(SM_P2), gt_sb[:, 24:32],
                    op0=ALU.mult, op1=ALU.add)))(),
                    reads=[B_gt, B_stats[SM_P2]], writes=[B_wg], tag="g_w")

            if "nogmove" in opts:
                for tau2 in range(TAU):
                    emit_gating(tau2)
            if b == 0 and tc == 0:
                emit_sx(1)
            if b == 0 and tc == 1 and stage == "full" and "nopre" not in opts:
                P.rebind(B_P + B_PT, [B_wq])
                P.dma("gpsimd", lambda g: g.dma_start(
                    out=Wq_sb[:, :].rearrange("p (dc o) -> p dc o", o=D),
                    in_=wq_in.rearrange("(dc p) o -> p dc o", p=128)),
                    writes=[B_wq], tag="wq_pre", klass="ld")
                wq_pre[0] = True

            def emit_ln2(tau):
                par = tau % 2
                tgl = tb + tau
                def _c2(tau=tau, tgl=tgl, par=par):
                    return lambda v: v.scalar_tensor_tensor(
                        tmp_sb2[par][:, :], x_tm[:, tgl * 1024:(tgl + 1) * 1024],
                        1.0, acc[:, tau * 1024:(tau + 1) * 1024],
                        op0=ALU.mult, op1=ALU.add, accum_out=stat(SM_L1S + par))
                P.op("vector", _c2(), reads=[B_xtm[tgl], B_acc[tau]],
                     writes=[B_tmp[par], B_stats[SM_L1S + par]], tag=f"c2_{tc}_{tau}")
                P.op("vector", (lambda par=par: (lambda v: v.tensor_scalar(
                    stat(SM_NMEAN + par), stat(SM_L1S + par), -1.0 / D, None,
                    op0=ALU.mult)))(),
                    reads=[B_stats[SM_L1S + par]], writes=[B_stats[SM_NMEAN + par]])
                P.op("scalar", (lambda par=par: (lambda s: s.activation(
                    psC[:, :], tmp_sb2[par][:, :], AF.Square,
                    bias=stat(SM_NMEAN + par), accum_out=stat(SM_VAR + par))))(),
                    reads=[B_tmp[par], B_stats[SM_NMEAN + par]],
                    writes=[B_stats[SM_VAR + par], PS[6], PS[7]])
                P.op("scalar", (lambda par=par: (lambda s: s.activation(
                    stat(SM_RSTD + par), stat(SM_VAR + par), AF.Ln,
                    bias=stat(SM_EPS), scale=1.0 / D)))(),
                    reads=[B_stats[SM_VAR + par], B_stats[SM_EPS]],
                    writes=[B_stats[SM_RSTD + par]])
                P.op("scalar", (lambda par=par: (lambda s: s.activation(
                    stat(SM_RSTD + par), stat(SM_RSTD + par), AF.Exp,
                    scale=-0.5)))(),
                    reads=[B_stats[SM_RSTD + par]], writes=[B_stats[SM_RSTD + par]])
                def _o2(tau=tau, par=par):
                    return lambda v: v.tensor_scalar(
                        acc[:, tau * 1024:(tau + 1) * 1024], tmp_sb2[par][:, :],
                        stat(SM_NMEAN + par), stat(SM_RSTD + par),
                        op0=ALU.add, op1=ALU.mult)
                P.op("vector", _o2(), reads=[B_tmp[par], B_stats[SM_NMEAN + par],
                                             B_stats[SM_RSTD + par]],
                     writes=[B_acc[tau]], tag=f"o2_{tc}_{tau}")


            e_order = list(range(NE)) if tc % 2 == 0 else list(range(NE - 1, -1, -1))
            for e_ in e_order:
                first_e = e_ == e_order[0]
                if wout_resident[0] != e_:
                    for ic in range(IC):
                        def _wo(e=e_, ic=ic):
                            return lambda s: s.dma_start_transpose(
                                woutT[:, ic * 1024:(ic + 1) * 1024], s_wout[e, ic])
                        P.dma("scalar", _wo(), reads=[B_swout[e_][ic]],
                              writes=[B_woutT[ic]], tag=f"woL{e_}", transpose=True)
                    wout_resident[0] = e_
                for oc2 in range(OC2):
                    sl = oc2 % 2
                    def _wl(e=e_, o=oc2, sl=sl):
                        return lambda s: s.dma_start(
                            out=winT[sl][:, :].rearrange("p (dc o) -> p dc o", o=256),
                            in_=s_winT[e, :, :, o * 256:(o + 1) * 256]
                                .rearrange("dc p o -> p dc o"))
                    P.dma("sync", _wl(), reads=[B_swinT[e_]],
                          writes=[B_winT[sl]], tag=f"wiL{e_}_{oc2}")
                    for sub in range(2):
                        oc = oc2 * 2 + sub
                        for tg in range(2):
                            bank01 = (oc * 2 + tg) % 2
                            bank = psS[:, bank01 * 512:(bank01 + 1) * 512]
                            bbuf = PS[bank01]
                            fns = []
                            for dc in range(DC):
                                def _mm(dc=dc, sub=sub, tg=tg, sl=sl, tc=tc,
                                        bank=bank):
                                    return lambda te: te.matmul(
                                        bank,
                                        winT[sl][:, dc * 256 + sub * 128:
                                                 dc * 256 + (sub + 1) * 128],
                                        XT[:, dc * 2048 + tc * 1024 + tg * 512:
                                           dc * 2048 + tc * 1024 + (tg + 1) * 512],
                                        start=(dc == 0), stop=(dc == DC - 1))
                                fns.append(_mm())
                            P.op("tensor", fns, reads=[B_winT[sl]] + B_X1T,
                                 writes=[bbuf], tag=f"uv{e_}_{oc}_{tg}")
                            if oc < IC:
                                def _ev(oc=oc, tg=tg, bank=bank):
                                    return lambda v: v.tensor_copy(
                                        u_buf[:, oc * 1024 + tg * 512:
                                              oc * 1024 + (tg + 1) * 512], bank)
                                P.op("vector", _ev(), reads=[bbuf],
                                     writes=[B_u[oc * 2 + tg]], tag=f"uev{e_}_{oc}")
                            else:
                                sv = (oc * 2 + tg) % 2
                                def _si(sv=sv, bank=bank):
                                    return lambda s: s.activation(
                                        sv_sb[sv][:, :], bank, AF.Silu)
                                P.op("scalar", _si(), reads=[bbuf],
                                     writes=[B_sv[sv]], tag=f"silu{e_}_{oc}")
                                uc = oc - IC
                                def _h(uc=uc, tg=tg, sv=sv):
                                    return lambda v: v.tensor_mul(
                                        u_buf[:, uc * 1024 + tg * 512:
                                              uc * 1024 + (tg + 1) * 512],
                                        u_buf[:, uc * 1024 + tg * 512:
                                              uc * 1024 + (tg + 1) * 512],
                                        sv_sb[sv][:, :])
                                P.op("vector", _h(), reads=[B_sv[sv],
                                                            B_u[uc * 2 + tg]],
                                     writes=[B_u[uc * 2 + tg]], tag=f"h{e_}_{uc}")
                if first_e and "nogmove" not in opts:
                    for tau2 in range(TAU):
                        emit_gating(tau2)
                if (b == 0 and tc == NTC - 1 and e_ == e_order[-1] and stage == "full" and "nopre" not in opts):
                    P.rebind(B_X1T, B_XT)
                    for dc in range(DC):
                        def _xt1(dc=dc):
                            return lambda sy: sy.dma_start_transpose(
                                XT[:, dc * 2048:(dc + 1) * 2048], s_x[1, dc])
                        P.dma("sync", _xt1(), reads=[B_sx[1][dc]],
                              writes=[B_XT[dc]], tag="XTpre", transpose=True)
                    xt_pre[0] = True
                for tau in range(TAU):
                    pr = tau % 2
                    banks = (psS[:, 1024:1536], psS[:, 1536:2048]) if pr == 0 \
                        else (psP4[:, :], psP5[:, :])
                    bbufs = [PS[2], PS[3]] if pr == 0 else [PS[4], PS[5]]
                    tg = tau // 4
                    fns = []
                    for ic in range(IC):
                        for half in range(2):
                            def _mm(ic=ic, half=half, tau=tau, banks=banks):
                                return lambda te: te.matmul(
                                    banks[half],
                                    u_buf[:, ic * 1024 + tau * 128:
                                          ic * 1024 + (tau + 1) * 128],
                                    woutT[:, ic * 1024 + half * 512:
                                          ic * 1024 + (half + 1) * 512],
                                    start=(ic == 0), stop=(ic == IC - 1))
                            fns.append(_mm())
                    P.op("tensor", fns,
                         reads=[B_u[ic2 * 2 + tg] for ic2 in range(IC)] + list(B_woutT),
                         writes=bbufs, tag=f"y{e_}_{tau}")
                    for half in range(2):
                        if first_e:
                            def _ac(tau=tau, half=half, e=e_, banks=banks):
                                return lambda v: v.tensor_scalar(
                                    acc[:, tau * 1024 + half * 512:
                                        tau * 1024 + (half + 1) * 512],
                                    banks[half],
                                    wg_sb[:, tau * 8 + e: tau * 8 + e + 1], None,
                                    op0=ALU.mult)
                        else:
                            def _ac(tau=tau, half=half, e=e_, banks=banks):
                                return lambda v: v.scalar_tensor_tensor(
                                    acc[:, tau * 1024 + half * 512:
                                        tau * 1024 + (half + 1) * 512],
                                    banks[half],
                                    wg_sb[:, tau * 8 + e: tau * 8 + e + 1],
                                    acc[:, tau * 1024 + half * 512:
                                        tau * 1024 + (half + 1) * 512],
                                    op0=ALU.mult, op1=ALU.add)
                        P.op("vector", _ac(), reads=[bbufs[half], B_wg, B_acc[tau]],
                             writes=[B_acc[tau]], tag=f"yac{e_}_{tau}_{half}")
                    if e_ == e_order[-1] and "noln2y" not in opts:
                        emit_ln2(tau)

            if "noln2y" in opts:
                for tau in range(TAU):
                    emit_ln2(tau)
            if b == 0:
                emit_xtm(1, tc)
            for h in range(2):
                def _st(b=b, tc=tc, h=h):
                    return lambda g: g.dma_start(
                        out=x_out[b, tc * 1024 + h * 512: tc * 1024 + (h + 1) * 512, :]
                            .rearrange("(t p) d -> p t d", p=128),
                        in_=acc[:, :].rearrange("p (t d) -> p t d", d=D)
                        [:, h * 4:(h + 1) * 4, :])
                P.dma("gpsimd", _st(), reads=B_acc[h * 4:(h + 1) * 4],
                      tag=f"out{b}_{tc}", klass="out")

    if "wdump" in opts:
        dbg = nc.dram_tensor("dbg", [NE, DC, 128, 2 * ISZ], BF16,
                             kind="ExternalOutput").ap()
        dbg0 = nc.dram_tensor("dbg0", [NE, 2 * ISZ, D], BF16,
                              kind="ExternalOutput").ap()
        for e in range(NE):
            P.dma("gpsimd", (lambda e=e: (lambda g: g.dma_start(
                out=dbg[e], in_=s_winT[e])))(),
                reads=[B_swinT[e]], tag="wdump", klass="out")
            P.dma("gpsimd", (lambda e=e: (lambda g: g.dma_start(
                out=dbg0[e], in_=s_win0[e])))(),
                reads=[B_swin0[e]], tag="wdump0", klass="out")
    for (q, klass), (sems, cnt) in P.QS.items():
        if q == "gpsimd" and cnt:
            for j in range(P.RROT):
                nj = (cnt - j + P.RROT - 1) // P.RROT
                if nj > 0:
                    P.raw_wait("gpsimd", sems[j], 16 * nj)

    with st:
        block = st.enter_context(nc.Block())
        P.emit(block)
    return nc


_NC_CACHE = {}


def _get_nc(stage, opts=frozenset()):
    key = (stage, tuple(sorted(opts)))
    if key not in _NC_CACHE:
        _NC_CACHE[key] = build(stage, opts)
    return _NC_CACHE[key]


def _consts():
    ident = np.eye(128, dtype=_BF)
    masks = np.zeros((4, 128, 512), dtype=np.float32)
    for r in range(4):
        for i in range(128):
            masks[r, i, 128 * r + i + 1:] = NEG
    return ident, masks.astype(_BF)


def kernel(L0_attn_x, L0_attn_Wq, L0_attn_Wk, L0_attn_Wv,
           L0_ffn_gate_w, L0_ffn_w_in, L0_ffn_w_out, stage="full",
           opts=frozenset()):
    x = np.ascontiguousarray(np.asarray(L0_attn_x, dtype=np.float32))
    wq = np.ascontiguousarray(np.asarray(L0_attn_Wq, dtype=np.float32))
    wk = np.ascontiguousarray(np.asarray(L0_attn_Wk, dtype=np.float32))
    wv = np.ascontiguousarray(np.asarray(L0_attn_Wv, dtype=np.float32))
    gw = np.ascontiguousarray(np.asarray(L0_ffn_gate_w, dtype=np.float32))
    win = np.ascontiguousarray(np.asarray(L0_ffn_w_in, dtype=np.float32))
    wout = np.ascontiguousarray(np.asarray(L0_ffn_w_out, dtype=np.float32))
    ident, masks = _consts()
    nc = _get_nc(stage, opts)
    in_maps = []
    for c in range(NCORES):
        in_maps.append({
            "x": x[NB * c: NB * (c + 1)],
            "wq": wq, "wk": wk, "wv": wv, "gw": gw,
            "win": win, "wout": wout,
            "ident": ident, "masks": masks,
        })
    res = run_bass_kernel_spmd(nc, in_maps, core_ids=list(range(NCORES)))
    out = np.concatenate([np.asarray(r["x_out"]) for r in res.results], axis=0)
    return out.astype(np.float32, copy=False)


# version 17
# speedup vs baseline: 1.1734x; 1.1734x over previous
import numpy as np
import ml_dtypes
import concourse.bass as bass
import concourse.mybir as mybir
from concourse.bass_utils import run_bass_kernel_spmd

F32 = mybir.dt.float32
BF16 = mybir.dt.bfloat16
AF = mybir.ActivationFunctionType
ALU = mybir.AluOpType
AX = mybir.AxisListType

NCORES = 8
NB = 2
S = 2048
D = 1024
NT = S // 128
DC = D // 128
NE = 8
ISZ = 1408
OC2 = 11
IC = ISZ // 128
TCH = 1024
NTC = S // TCH
TAU = TCH // 128
NEG = -1e30

_BF = ml_dtypes.bfloat16


class Tok:
    __slots__ = ("kind", "eng", "ord")
    def __init__(self, kind, eng, o):
        self.kind, self.eng, self.ord = kind, eng, o


class Buf:
    __slots__ = ("name", "w", "r", "psum")
    def __init__(self, name, psum=False):
        self.name, self.w, self.r, self.psum = name, None, [], psum


class Prog:
    ENGS = ("tensor", "vector", "scalar", "gpsimd", "sync")
    DMAQ = ("gpsimd", "sync", "scalar")

    def __init__(self, nc, stack):
        self.nc = nc
        self.stack = stack
        self.recs = {e: [] for e in self.ENGS}
        self.n = {e: 0 for e in self.ENGS}
        self.SE = {e: stack.enter_context(nc.semaphore(f"se_{e}"))
                   for e in self.ENGS}
        self.QS = {}
        self.wm = {e: {} for e in self.ENGS}

    RROT = 6

    def _qs(self, q, klass):
        k = (q, klass)
        if k not in self.QS:
            self.QS[k] = [[self.stack.enter_context(
                self.nc.semaphore(f"q_{q}_{klass}{j}"))
                for j in range(self.RROT)], 0]
        return self.QS[k]

    def _deps_from(self, eng, reads, writes):
        deps = []
        for b in reads:
            if b.w is not None:
                deps.append(b.w)
            if b.psum:
                deps.extend(t for t in b.r if t.eng != eng)
        for b in writes:
            if b.w is not None:
                deps.append(b.w)
            deps.extend(b.r)
        return deps

    def _waits(self, eng, deps):
        best = {}
        for t in deps:
            if t.kind == "e":
                if t.eng == eng and eng == "tensor":
                    continue
                sem, val = self.SE[t.eng], t.ord + 1
            else:
                sems = self.QS[t.eng][0]
                sem = sems[t.ord % self.RROT]
                val = 16 * (t.ord // self.RROT + 1)
            k = id(sem)
            if val > best.get(k, (None, -1))[1]:
                best[k] = (sem, val)
        out = []
        wm = self.wm[eng]
        for k, (sem, val) in best.items():
            if val > wm.get(k, 0):
                out.append((sem, val))
                wm[k] = val
        return out

    def _upd(self, tok, reads, writes):
        for b in reads:
            b.r.append(tok)
        for b in writes:
            b.w = tok
            b.r = []

    def op(self, eng, fns, reads=(), writes=(), tag=""):
        if not isinstance(fns, (list, tuple)):
            fns = [fns]
        deps = self._deps_from(eng, reads, writes)
        waits = self._waits(eng, deps)
        tok = Tok("e", eng, self.n[eng])
        self.n[eng] += 1
        self.recs[eng].append((waits, list(fns), (self.SE[eng], 1), tag))
        self._upd(tok, reads, writes)
        return tok

    def dma(self, q, fn, reads=(), writes=(), tag="", transpose=False,
            klass="d"):
        if transpose:
            klass = "t"
        deps = self._deps_from(q, reads, writes)
        waits = self._waits(q, deps)
        qs = self._qs(q, klass)
        tok = Tok("d", (q, klass), qs[1])
        sem = qs[0][qs[1] % self.RROT]
        qs[1] += 1
        self.recs[q].append((waits, [fn], (sem, 16), tag))
        self._upd(tok, reads, writes)
        return tok

    def raw_wait(self, eng, sem, val):
        self.recs[eng].append(([(sem, val)], [], None, "final"))

    def dma_barrier(self, q, bufs, klass="d"):
        qs = self._qs(q, klass)
        waits = []
        for j in range(self.RROT):
            nj = (qs[1] - j + self.RROT - 1) // self.RROT
            if nj > 0:
                waits.append((qs[0][j], 16 * nj))
                self.wm[q][id(qs[0][j])] = max(
                    self.wm[q].get(id(qs[0][j]), 0), 16 * nj)
        tok = Tok("e", q, self.n[q])
        self.n[q] += 1
        self.recs[q].append((waits, [lambda e: e.nop()],
                             (self.SE[q], 1), "dmabar"))
        for b in bufs:
            b.w, b.r = tok, []
        return tok

    def rebind(self, old_bufs, new_bufs):
        toks = []
        for b in old_bufs:
            if b.w is not None:
                toks.append(b.w)
            toks.extend(b.r)
            b.w, b.r = None, []
        for b in new_bufs:
            b.r = b.r + toks

    def emit(self, block):
        def body_for(eng_name):
            recs = self.recs[eng_name]
            def body(e):
                for waits, fns, inc, _tag in recs:
                    for sem, val in waits:
                        e.wait_ge(sem, val)
                    for i, fn in enumerate(fns):
                        ins = fn(e)
                        if inc is not None and i == len(fns) - 1:
                            ins.then_inc(inc[0], inc[1])
            return body
        block.tensor(body_for("tensor"))
        block.vector(body_for("vector"))
        block.scalar(body_for("scalar"))
        block.gpsimd(body_for("gpsimd"))
        block.sync(body_for("sync"))


def build(stage="full", opts=frozenset()):
    from contextlib import ExitStack
    nc = bass.Bass("TRN2", target_bir_lowering=False)
    st = ExitStack()

    x_in = nc.dram_tensor("x", [NB, S, D], F32, kind="ExternalInput").ap()
    wq_in = nc.dram_tensor("wq", [D, D], F32, kind="ExternalInput").ap()
    wk_in = nc.dram_tensor("wk", [D, D], F32, kind="ExternalInput").ap()
    wv_in = nc.dram_tensor("wv", [D, D], F32, kind="ExternalInput").ap()
    gw_in = nc.dram_tensor("gw", [NE, D], F32, kind="ExternalInput").ap()
    win_in = nc.dram_tensor("win", [NE, 2 * ISZ, D], F32, kind="ExternalInput").ap()
    wout_in = nc.dram_tensor("wout", [NE, D, ISZ], F32, kind="ExternalInput").ap()
    id_in = nc.dram_tensor("ident", [128, 128], BF16, kind="ExternalInput").ap()
    mk_in = nc.dram_tensor("masks", [4, 128, 512], BF16, kind="ExternalInput").ap()
    x_out = nc.dram_tensor("x_out", [NB, S, D], F32, kind="ExternalOutput").ap()

    s_win0 = nc.dram_tensor("s_win0", [NE, 2 * ISZ, D], BF16, kind="Internal").ap()
    s_winT = nc.dram_tensor("s_winT", [NE, DC, 128, 2 * ISZ], BF16, kind="Internal").ap()
    s_wout = nc.dram_tensor("s_wout", [NE, IC, D, 128], BF16, kind="Internal").ap()
    s_x = nc.dram_tensor("s_x", [NB, DC, S, 128], BF16, kind="Internal").ap()

    sb_top = nc._sbuf_addr_for_side("right")
    sb_base = 17472
    assert sb_base >= nc._sbuf_addr_for_side("left") + 900, (
        sb_base, nc._sbuf_addr_for_side("left"))
    off = [sb_base]
    def alloc(name, cols, dt, parts=128, at=None):
        nbytes = cols * (2 if dt == BF16 else 4)
        if at is None:
            at = (off[0] + 31) & ~31
            off[0] = at + nbytes
        t = nc.alloc_sbuf_tensor_at(name, [parts, cols], dt, offset=at)
        return t

    id_sb = alloc("id_sb", 128, BF16)
    mask_sb = alloc("mask_sb", 4 * 512, BF16)
    gwT_sb = alloc("gwT_sb", DC * 8, BF16)
    gw_bf = alloc("gw_bf", D, BF16, parts=8)
    stats = alloc("stats", 64, F32)
    wg_sb = alloc("wg_sb", 64, F32)
    gt_sb = alloc("gt_sb", 64, F32)

    base = (off[0] + 63) & ~63
    R0 = base
    R1 = R0 + 32768
    R2 = R1 + 32768
    R3 = R2 + 32768
    R4 = R3 + 32768
    R5 = R4 + 32768
    END = R5 + 16384
    x_tm = alloc("x_tm", 16 * 1024, BF16, at=R0)
    XT = alloc("XT", 8 * 2048, BF16, at=R1)
    QT = alloc("QT", 8 * 2048, BF16, at=R2)
    acc = alloc("acc", 8 * 1024, F32, at=R2)
    stg_i = alloc("stg_i", 2816, BF16, at=END)
    stg_o = alloc("stg_o", 2816, BF16, at=END + 5632)
    KT = alloc("KT", 8 * 2048, BF16, at=R3)
    u_buf = alloc("u_buf", 11 * 1024, BF16, at=R3)
    sv_sb = [alloc(f"sv{k}", 512, BF16, at=R3 + 22528 + 1024 * k) for k in range(2)]
    V_tm = alloc("V_tm", 16 * 1024, BF16, at=R4)
    winT = [alloc(f"winT{k}", 8 * 256, BF16, at=R4 + 4096 * k) for k in range(2)]
    woutT = alloc("woutT", 11 * 1024, BF16, at=R4 + 8192)
    Wq_sb = alloc("Wq_sb", 8 * 1024, BF16, at=R5)
    P_sb = alloc("P_sb", 2 * 2048, BF16, at=R5)
    PT_sb = alloc("PT_sb", 16 * 128, BF16, at=R5 + 8192)
    sq_sb = alloc("sq_sb", 1024, F32, at=END + 11264)
    tmp_sb2 = [alloc(f"tmp_sb{k}", 1024, F32, at=END + 15360 + 4096 * k)
               for k in range(2)]
    assert END + 23552 <= sb_top, (END, sb_top)

    psS = nc.alloc_psum_tensor("psS", [128, 2048], F32)
    psP4 = nc.alloc_psum_tensor("psP4", [128, 512], F32)
    psP5 = nc.alloc_psum_tensor("psP5", [128, 512], F32)
    psC = nc.alloc_psum_tensor("psC", [128, 1024], F32)

    P = Prog(nc, st)

    PS = [Buf(f"ps{i}", psum=True) for i in range(8)]
    def ps_bufs(lo_bank, hi_bank):
        return [PS[i] for i in range(lo_bank, hi_bank + 1)]
    B_id = Buf("id"); B_mask = Buf("mask"); B_gwT = Buf("gwT"); B_gwbf = Buf("gwbf")
    B_stats = [Buf(f"st{i}") for i in range(24)]
    B_wg = Buf("wg"); B_gt = Buf("gt")
    B_stgi = [Buf("stgi")]
    B_stgo = [Buf("stgo")]
    B_stg4 = [Buf(f"stg4_{k}") for k in range(4)]
    B_xtm = [Buf(f"xtm{t}") for t in range(16)]
    B_XT = [Buf(f"XT{c}") for c in range(8)]
    B_X1T = [Buf(f"X1T{c}") for c in range(8)]
    B_QT = [Buf(f"QT{c}") for c in range(8)]
    B_acc = [Buf(f"acc{t}") for t in range(TAU)]
    B_KT = [Buf(f"KT{c}") for c in range(8)]
    B_u = [Buf(f"u{o}") for o in range(22)]
    B_sv = [Buf(f"sv{k}") for k in range(2)]
    B_V = [Buf(f"V{t}") for t in range(16)]
    B_winT = [Buf(f"winT{k}") for k in range(2)]
    B_woutT = [Buf(f"woutT{i}") for i in range(IC)]
    B_wq = Buf("wq")
    B_P = [Buf("P0"), Buf("P1")]
    B_PT = [Buf(f"PT{c}") for c in range(16)]
    B_tmp = [Buf("tmp0"), Buf("tmp1")]
    B_swin0 = [Buf(f"sw0_{e}") for e in range(NE)]
    B_swinT = [Buf(f"swT{e}") for e in range(NE)]
    B_swout = [[Buf(f"swo{e}_{i}") for i in range(IC)] for e in range(NE)]
    B_sx = [[Buf(f"sx{b}_{c}") for c in range(DC)] for b in range(NB)]

    def stat(c):
        return stats[:, c:c + 1]
    SM_NMAX = 0
    SM_RSUM = 2
    SM_RINV = 4
    SM_L1S = 6
    SM_NMEAN = 8
    SM_VAR = 10
    SM_RSTD = 12
    SM_EPS = 14
    SM_M1 = 16
    SM_M2 = 18
    SM_P1 = 20
    SM_P2 = 22

    P.dma("sync", lambda e: e.dma_start(out=id_sb[:, :], in_=id_in), writes=[B_id])
    P.dma("sync", lambda e: e.dma_start(
        out=mask_sb[:, :].rearrange("p (r k) -> p r k", k=512),
        in_=mk_in.rearrange("r p k -> p r k")), writes=[B_mask])
    P.op("vector", lambda e: e.memset(stats[:, SM_EPS:SM_EPS + 2], 1e-5),
         writes=[B_stats[SM_EPS]])

    def emit_sx(b, extra_fns=(), extra_bufs=()):
        for dc in range(DC):
            for h in range(2):
                def _xs(b=b, dc=dc, h=h):
                    return lambda g: g.dma_start(
                        out=s_x[b, dc, h * 1024:(h + 1) * 1024, :],
                        in_=x_in[b, h * 1024:(h + 1) * 1024,
                                 dc * 128:(dc + 1) * 128])
                P.dma("gpsimd", _xs(), writes=[B_sx[b][dc]], tag="x_s", klass="cast")
        for fn, bufs in zip(extra_fns, extra_bufs):
            P.dma("gpsimd", fn, writes=bufs, tag="x_s_extra", klass="cast")
        P.dma_barrier("gpsimd", [B_sx[b][dc] for dc in range(DC)]
                      + [bb for bufs in extra_bufs for bb in bufs],
                      klass="cast")

    emit_sx(0, extra_fns=[lambda e: e.dma_start(out=gw_bf[:, :], in_=gw_in)],
            extra_bufs=[[B_gwbf]])

    def _gw_tp(dc):
        return lambda e: e.matmul(psP4[:, dc * 8:(dc + 1) * 8],
                                  gw_bf[0:8, dc * 128:(dc + 1) * 128],
                                  id_sb[0:8, 0:8], start=True, stop=True)
    P.op("tensor", [_gw_tp(dc) for dc in range(DC)],
         reads=[B_gwbf, B_id], writes=[PS[4]], tag="gwT")
    P.op("vector", lambda e: e.tensor_copy(gwT_sb[:, :], psP4[:, 0:64]),
         reads=[PS[4]], writes=[B_gwT], tag="gwT_ev")

    def emit_casts(experts):
        for e in experts:
            for q in range(4):
                def _ca(e=e, q=q):
                    return lambda g: g.dma_start(
                        out=s_win0[e, q * 704:(q + 1) * 704, :],
                        in_=win_in[e, q * 704:(q + 1) * 704, :])
                P.dma("gpsimd", _ca(), writes=[B_swin0[e]], tag="wi_cast", klass="cast")
            P.dma_barrier("gpsimd", [B_swin0[e]], klass="cast")

    def emit_wo_casts():
        for e in range(NE):
            for i in range(IC):
                def _wo(e=e, i=i):
                    return lambda g: g.dma_start(
                        out=s_wout[e, i],
                        in_=wout_in[e, :, i * 128:(i + 1) * 128])
                P.dma("gpsimd", _wo(), writes=[B_swout[e][i]], tag="wo_cast", klass="cast")
        P.dma_barrier("gpsimd", [b2 for row in B_swout for b2 in row], klass="cast")

    class Chain:
        ins = [stg_i[:, 0:1408], stg_i[:, 1408:2816]]
        outs = [stg_o[:, 0:1408], stg_o[:, 1408:2816]]
        bins = [B_stg4[0], B_stg4[1]]
        bouts = [B_stg4[2], B_stg4[3]]

        def __init__(self, e):
            self.e = e
            self.k = 0
            self.pend = []

        def piece(self):
            if self.k < 2 * DC:
                k, e = self.k, self.e
                dc, h, sl = k // 2, k % 2, k % 2
                def _tp(e=e, dc=dc, h=h, inb=self.ins[sl]):
                    return lambda s: s.dma_start_transpose(
                        inb, s_win0[e, h * 1408:(h + 1) * 1408,
                                    dc * 128:(dc + 1) * 128])
                P.dma("sync", _tp(), reads=[B_swin0[e]], writes=[self.bins[sl]],
                      tag="wi_tp", transpose=True)
                eng = "vector" if k % 2 == 0 else "scalar"
                def _cp(inb=self.ins[sl], outb=self.outs[sl], eng=eng):
                    def f(v):
                        if eng == "vector":
                            return v.tensor_copy(outb, inb)
                        return v.copy(outb, inb)
                    return f
                P.op(eng, _cp(), reads=[self.bins[sl]], writes=[self.bouts[sl]],
                     tag="wi_cp")
                self.pend.append(k)
                self.k += 1
                if self.k >= 2:
                    self._store()
                return True
            if self.pend:
                self._store()
                return True
            return False

        def _store(self):
            k = self.pend.pop(0)
            e = self.e
            dc, h, sl = k // 2, k % 2, k % 2
            def _st(e=e, dc=dc, h=h, outb=self.outs[sl]):
                return lambda s: s.dma_start(
                    out=s_winT[e, dc, :, h * 1408:(h + 1) * 1408], in_=outb)
            P.dma("sync", _st(), reads=[self.bouts[sl]], writes=[B_swinT[e]],
                  tag="wi_st", klass="st")

    def emit_chain(e):
        c = Chain(e)
        while c.piece():
            pass


    xt_pre = [False]
    wq_pre = [False]
    wout_resident = [-1]
    for b in range(NB):
        if b > 0:
            if not xt_pre[0]:
                P.rebind(B_X1T, B_XT)
            P.rebind(B_acc, B_QT)
            P.rebind(B_u + B_sv, B_KT)
            P.rebind(B_winT + B_woutT, B_V)
            if not wq_pre[0]:
                P.rebind(B_P + B_PT, [B_wq])

        def emit_xtm(b2, h):
            def _xl(b2=b2, h=h):
                return lambda g: g.dma_start(
                    out=x_tm[:, :].rearrange("p (t d) -> p t d", d=D)
                    [:, h * 8:(h + 1) * 8, :],
                    in_=x_in[b2, h * 1024:(h + 1) * 1024, :]
                        .rearrange("(t p) d -> p t d", p=128))
            P.dma("gpsimd", _xl(), writes=B_xtm[h * 8:(h + 1) * 8], tag="x_tm", klass="ld")
        if b == 0:
            for h in range(2):
                emit_xtm(0, h)
        elif stage == "attn":
            emit_sx(b)
            for h in range(2):
                emit_xtm(b, h)
        if not xt_pre[0]:
            for dc in range(DC):
                def _xt(b=b, dc=dc):
                    return lambda s: s.dma_start_transpose(
                        XT[:, dc * 2048:(dc + 1) * 2048], s_x[b, dc])
                P.dma("sync", _xt(), reads=[B_sx[b][dc]], writes=[B_XT[dc]],
                      tag="XT", transpose=True)
        xt_pre[0] = False

        front_chains = []
        if b == 0:
            P.dma("gpsimd", lambda g: g.dma_start(
                out=Wq_sb[:, :].rearrange("p (dc o) -> p dc o", o=D),
                in_=wq_in.rearrange("(dc p) o -> p dc o", p=128)),
                writes=[B_wq], tag="wq0", klass="ld")
            wq_pre[0] = True
            emit_casts([0, 1])
            front_chains = [Chain(0), Chain(1)]
        for wi, (w_ap, dst, dstbufs, kind) in enumerate([
                (wq_in, QT, B_QT, "q"), (wk_in, KT, B_KT, "k"),
                (wv_in, V_tm, B_V, "v")]):
            if kind == "q" and wq_pre[0]:
                wq_pre[0] = False
            else:
                def _wl(w_ap=w_ap):
                    return lambda g: g.dma_start(
                        out=Wq_sb[:, :].rearrange("p (dc o) -> p dc o", o=D),
                        in_=w_ap.rearrange("(dc p) o -> p dc o", p=128))
                P.dma("gpsimd", _wl(), writes=[B_wq], tag=f"w{kind}_l", klass="ld")
            if kind in ("q", "k"):
                for mo in range(8):
                    for tg in range(4):
                        bank = psP4 if (mo * 4 + tg) % 2 == 0 else psP5
                        bbuf = PS[4] if (mo * 4 + tg) % 2 == 0 else PS[5]
                        fns = []
                        for dc in range(DC):
                            def _mm(dc=dc, mo=mo, tg=tg, bank=bank):
                                return lambda t: t.matmul(
                                    bank[:, :],
                                    Wq_sb[:, dc * D + mo * 128: dc * D + (mo + 1) * 128],
                                    XT[:, dc * 2048 + tg * 512: dc * 2048 + (tg + 1) * 512],
                                    start=(dc == 0), stop=(dc == DC - 1))
                            fns.append(_mm())
                        P.op("tensor", fns, reads=[B_wq] + B_XT, writes=[bbuf],
                             tag=f"{kind}gen")
                        def _ev(mo=mo, tg=tg, bank=bank, dst=dst):
                            return lambda v: v.tensor_copy(
                                dst[:, mo * 2048 + tg * 512: mo * 2048 + (tg + 1) * 512],
                                bank[:, :])
                        P.op("vector", _ev(), reads=[bbuf], writes=[dstbufs[mo]],
                             tag=f"{kind}ev")
            else:
                for tt in range(16):
                    for eh in range(2):
                        bank = psP4 if (tt * 2 + eh) % 2 == 0 else psP5
                        bbuf = PS[4] if (tt * 2 + eh) % 2 == 0 else PS[5]
                        fns = []
                        for dc in range(DC):
                            def _mm(dc=dc, tt=tt, eh=eh, bank=bank):
                                return lambda t: t.matmul(
                                    bank[:, :],
                                    XT[:, dc * 2048 + tt * 128: dc * 2048 + (tt + 1) * 128],
                                    Wq_sb[:, dc * D + eh * 512: dc * D + (eh + 1) * 512],
                                    start=(dc == 0), stop=(dc == DC - 1))
                            fns.append(_mm())
                        P.op("tensor", fns, reads=[B_wq] + B_XT, writes=[bbuf],
                             tag="vgen")
                        def _ev(tt=tt, eh=eh, bank=bank):
                            return lambda v: v.tensor_copy(
                                V_tm[:, tt * 1024 + eh * 512: tt * 1024 + (eh + 1) * 512],
                                bank[:, :])
                        P.op("vector", _ev(), reads=[bbuf], writes=[B_V[tt]],
                             tag="vev")

        P.rebind(B_XT, B_X1T)
        P.rebind([B_wq], B_P + B_PT)

        def emit_scores(t):
            jd = t // 4
            r = t % 4
            W = 512 * (jd + 1)
            par = t % 2
            for kg in range(jd + 1):
                fns = []
                for dc in range(DC):
                    def _mm(dc=dc, kg=kg, t=t):
                        return lambda te: te.matmul(
                            psS[:, kg * 512:(kg + 1) * 512],
                            QT[:, dc * 2048 + t * 128: dc * 2048 + (t + 1) * 128],
                            KT[:, dc * 2048 + kg * 512: dc * 2048 + (kg + 1) * 512],
                            start=(dc == 0), stop=False)
                    fns.append(_mm())
                if kg == jd:
                    def _msk(kg=kg, r=r):
                        return lambda te: te.matmul(
                            psS[:, kg * 512:(kg + 1) * 512], id_sb[:, :],
                            mask_sb[:, r * 512:(r + 1) * 512],
                            start=False, stop=True)
                    fns.append(_msk())
                else:
                    def _mm_last(dc=DC - 1, kg=kg, t=t):
                        return lambda te: te.matmul(
                            psS[:, kg * 512:(kg + 1) * 512],
                            QT[:, dc * 2048 + t * 128: dc * 2048 + (t + 1) * 128],
                            KT[:, dc * 2048 + kg * 512: dc * 2048 + (kg + 1) * 512],
                            start=False, stop=True)
                    fns[-1] = _mm_last()
                P.op("tensor", fns, reads=B_QT + B_KT + [B_id, B_mask],
                     writes=[PS[kg]], tag=f"sc{t}_{kg}")
            def _nm(t=t, W=W, par=par):
                return lambda v: v.reduce_max(stat(SM_NMAX + par), psS[:, 0:W],
                                              axis=AX.X, negate=True)
            P.op("vector", _nm(), reads=ps_bufs(0, jd),
                 writes=[B_stats[SM_NMAX + par]], tag=f"nmax{t}")
            def _nms(par=par):
                return lambda v: v.tensor_scalar(
                    stat(SM_NMAX + par), stat(SM_NMAX + par), 1.0 / 32.0, None,
                    op0=ALU.mult)
            P.op("vector", _nms(), reads=[B_stats[SM_NMAX + par]],
                 writes=[B_stats[SM_NMAX + par]], tag=f"nmsc{t}")
            def _exp(t=t, W=W, par=par):
                return lambda s: s.activation(
                    P_sb[:, par * 2048: par * 2048 + W], psS[:, 0:W], AF.Exp,
                    bias=stat(SM_NMAX + par), scale=1.0 / 32.0,
                    accum_out=stat(SM_RSUM + par))
            P.op("scalar", _exp(), reads=ps_bufs(0, jd) + [B_stats[SM_NMAX + par]],
                 writes=[B_P[par], B_stats[SM_RSUM + par]], tag=f"exp{t}")

        def emit_consume(t):
            par = t % 2
            def _rinv(par=par):
                return lambda v: v.reciprocal(stat(SM_RINV + par),
                                              stat(SM_RSUM + par))
            P.op("vector", _rinv(), reads=[B_stats[SM_RSUM + par]],
                 writes=[B_stats[SM_RINV + par]], tag=f"rinv{t}")
            nch = t + 1
            for rnd in range((nch + 3) // 4):
                c0 = rnd * 4
                n = min(4, nch - c0)
                bank = psP4 if rnd % 2 == 0 else psP5
                bbuf = PS[4] if rnd % 2 == 0 else PS[5]
                fns = []
                for j in range(n):
                    def _tp(j=j, c=c0 + j, bank=bank, par=par):
                        return lambda te: te.matmul(
                            bank[:, j * 128:(j + 1) * 128],
                            P_sb[:, par * 2048 + c * 128: par * 2048 + (c + 1) * 128],
                            id_sb[:, :], start=True, stop=True)
                    fns.append(_tp())
                P.op("tensor", fns, reads=[B_P[par], B_id], writes=[bbuf],
                     tag=f"pt{t}_{rnd}")
                def _ev(c0=c0, n=n, bank=bank):
                    return lambda v: v.tensor_copy(
                        PT_sb[:, c0 * 128:(c0 + n) * 128], bank[:, 0:n * 128])
                P.op("vector", _ev(), reads=[bbuf],
                     writes=B_PT[c0:c0 + n], tag=f"ptev{t}_{rnd}")
            fns = []
            for c in range(nch):
                for half in range(2):
                    def _mm(c=c, half=half):
                        return lambda te: te.matmul(
                            psC[:, half * 512:(half + 1) * 512],
                            PT_sb[:, c * 128:(c + 1) * 128],
                            V_tm[:, c * 1024 + half * 512: c * 1024 + (half + 1) * 512],
                            start=(c == 0), stop=(c == nch - 1))
                    fns.append(_mm())
            P.op("tensor", fns, reads=B_PT[0:nch] + B_V[0:nch],
                 writes=[PS[6], PS[7]], tag=f"ctx{t}")
            def _stt(t=t, par=par):
                return lambda v: v.scalar_tensor_tensor(
                    tmp_sb2[par][:, :], psC[:, :], stat(SM_RINV + par),
                    x_tm[:, t * 1024:(t + 1) * 1024],
                    op0=ALU.mult, op1=ALU.add, accum_out=stat(SM_L1S + par))
            P.op("vector", _stt(), reads=[PS[6], PS[7], B_stats[SM_RINV + par],
                                          B_xtm[t]],
                 writes=[B_tmp[par], B_stats[SM_L1S + par]], tag=f"stt{t}")
            def _nmean(par=par):
                return lambda v: v.tensor_scalar(
                    stat(SM_NMEAN + par), stat(SM_L1S + par), -1.0 / D, None,
                    op0=ALU.mult)
            P.op("vector", _nmean(), reads=[B_stats[SM_L1S + par]],
                 writes=[B_stats[SM_NMEAN + par]], tag=f"nmean{t}")
            def _sq(par=par):
                return lambda s: s.activation(
                    sq_sb[:, :], tmp_sb2[par][:, :], AF.Square,
                    bias=stat(SM_NMEAN + par), accum_out=stat(SM_VAR + par))
            P.op("scalar", _sq(), reads=[B_tmp[par], B_stats[SM_NMEAN + par]],
                 writes=[B_stats[SM_VAR + par]], tag=f"sq{t}")
            def _rln(par=par):
                return lambda s: s.activation(
                    stat(SM_RSTD + par), stat(SM_VAR + par), AF.Ln,
                    bias=stat(SM_EPS), scale=1.0 / D)
            P.op("scalar", _rln(), reads=[B_stats[SM_VAR + par], B_stats[SM_EPS]],
                 writes=[B_stats[SM_RSTD + par]], tag=f"rln{t}")
            def _rex(par=par):
                return lambda s: s.activation(
                    stat(SM_RSTD + par), stat(SM_RSTD + par), AF.Exp, scale=-0.5)
            P.op("scalar", _rex(), reads=[B_stats[SM_RSTD + par]],
                 writes=[B_stats[SM_RSTD + par]], tag=f"rex{t}")
            def _x1(t=t, par=par):
                return lambda v: v.tensor_scalar(
                    x_tm[:, t * 1024:(t + 1) * 1024], tmp_sb2[par][:, :],
                    stat(SM_NMEAN + par), stat(SM_RSTD + par),
                    op0=ALU.add, op1=ALU.mult)
            P.op("vector", _x1(), reads=[B_tmp[par], B_stats[SM_NMEAN + par],
                                         B_stats[SM_RSTD + par]],
                 writes=[B_xtm[t]], tag=f"x1{t}")

        def emit_x1t(t):
            for rnd in range(2):
                bank = psP5 if rnd % 2 == 0 else psP4
                bbuf = PS[5] if rnd % 2 == 0 else PS[4]
                fns = []
                for j in range(4):
                    dc = rnd * 4 + j
                    def _tp(dc=dc, j=j, t=t, bank=bank):
                        return lambda te: te.matmul(
                            bank[:, j * 128:(j + 1) * 128],
                            x_tm[:, t * 1024 + dc * 128: t * 1024 + (dc + 1) * 128],
                            id_sb[:, :], start=True, stop=True)
                    fns.append(_tp())
                P.op("tensor", fns, reads=[B_xtm[t], B_id], writes=[bbuf],
                     tag=f"x1t{t}_{rnd}")
                def _ev(rnd=rnd, t=t, bank=bank):
                    def f(v):
                        dst = XT[:, :].rearrange("p (dc t2) -> p dc t2", t2=2048)[
                            :, rnd * 4:(rnd + 1) * 4, t * 128:(t + 1) * 128]
                        src = bank[:, :].rearrange("p (j o) -> p j o", o=128)
                        return v.tensor_copy(dst, src)
                    return f
                P.op("vector", _ev(), reads=[bbuf],
                     writes=[B_X1T[rnd * 4 + j2] for j2 in range(4)],
                     tag=f"x1tev{t}_{rnd}")

        for t in range(NT + 2):
            if t < NT:
                emit_scores(t)
            if t >= 1 and t - 1 < NT:
                emit_consume(t - 1)
            if t >= 2:
                emit_x1t(t - 2)
            if t == 3 and b == 0:
                emit_casts(range(2, NE))
                emit_wo_casts()
            if t >= 9 and front_chains:
                for _ in range(4):
                    if front_chains and not front_chains[0].piece():
                        front_chains.pop(0)
        while front_chains:
            if not front_chains[0].piece():
                front_chains.pop(0)

        if stage == "attn":
            for h in range(2):
                def _o(b=b, h=h):
                    return lambda g: g.dma_start(
                        out=x_out[b, h * 1024:(h + 1) * 1024, :]
                            .rearrange("(t p) d -> p t d", p=128),
                        in_=x_tm[:, :].rearrange("p (t d) -> p t d", d=D)
                        [:, h * 8:(h + 1) * 8, :])
                P.dma("gpsimd", _o(), reads=B_xtm[h * 8:(h + 1) * 8], tag="dbg_out", klass="out")
            continue

        wout_resident[0] = -1
        P.rebind(B_QT, B_acc)
        P.rebind(B_KT, B_u + B_sv)
        P.rebind(B_V, B_winT + B_woutT)

        for tc in range(NTC):
            tb = tc * TAU
            for tau in range(TAU):
                fns = []
                for dc in range(DC):
                    def _mm(dc=dc, tau=tau, tb=tb):
                        return lambda te: te.matmul(
                            psC[:, tau * 8:(tau + 1) * 8],
                            XT[:, dc * 2048 + (tb + tau) * 128:
                               dc * 2048 + (tb + tau + 1) * 128],
                            gwT_sb[:, dc * 8:(dc + 1) * 8],
                            start=(dc == 0), stop=(dc == DC - 1))
                    fns.append(_mm())
                P.op("tensor", fns, reads=B_X1T + [B_gwT], writes=[PS[6]],
                     tag=f"gate{tc}_{tau}")
            P.op("vector", lambda e: e.tensor_copy(wg_sb[:, :], psC[:, 0:64]),
                 reads=[PS[6]], writes=[B_wg], tag=f"gatev{tc}")
            def emit_gating(tau):
                g8 = wg_sb[:, tau * 8:(tau + 1) * 8]
                P.op("vector", (lambda g8=g8: (lambda v: v.reduce_max(
                    stat(SM_M1), g8, axis=AX.X)))(),
                    reads=[B_wg], writes=[B_stats[SM_M1]], tag="g_m1")
                P.op("vector", (lambda g8=g8: (lambda v: v.tensor_scalar(
                    gt_sb[:, 0:8], g8, stat(SM_M1), None, op0=ALU.is_equal)))(),
                    reads=[B_wg, B_stats[SM_M1]], writes=[B_gt], tag="g_eq1")
                P.op("vector", (lambda g8=g8: (lambda v: v.scalar_tensor_tensor(
                    gt_sb[:, 8:16], gt_sb[:, 0:8], NEG, g8,
                    op0=ALU.mult, op1=ALU.add)))(),
                    reads=[B_gt, B_wg], writes=[B_gt], tag="g_gm")
                P.op("vector", lambda v: v.reduce_max(stat(SM_M2), gt_sb[:, 8:16],
                                                      axis=AX.X),
                     reads=[B_gt], writes=[B_stats[SM_M2]], tag="g_m2")
                P.op("vector", lambda v: v.tensor_scalar(
                    stat(SM_M2 + 1), stat(SM_M1), stat(SM_M2), None,
                    op0=ALU.subtract),
                    reads=[B_stats[SM_M2], B_stats[SM_M1]],
                    writes=[B_stats[SM_M2 + 1]], tag="g_d")
                P.op("scalar", lambda s: s.activation(
                    stat(SM_P2), stat(SM_M2 + 1), AF.Exp),
                    reads=[B_stats[SM_M2 + 1]], writes=[B_stats[SM_P2]], tag="g_q")
                P.op("vector", lambda v: v.tensor_scalar(
                    stat(SM_P2), stat(SM_P2), 1.0, None, op0=ALU.add),
                    reads=[B_stats[SM_P2]], writes=[B_stats[SM_P2]], tag="g_1q")
                P.op("vector", lambda v: v.reciprocal(stat(SM_P2), stat(SM_P2)),
                     reads=[B_stats[SM_P2]], writes=[B_stats[SM_P2]], tag="g_p2")
                P.op("vector", lambda v: v.tensor_scalar(
                    stat(SM_P1), stat(SM_P2), -1.0, 1.0, op0=ALU.mult, op1=ALU.add),
                    reads=[B_stats[SM_P2]], writes=[B_stats[SM_P1]], tag="g_p1")
                P.op("vector", lambda v: v.tensor_scalar(
                    gt_sb[:, 16:24], gt_sb[:, 8:16], stat(SM_M2), None,
                    op0=ALU.is_equal),
                    reads=[B_gt, B_stats[SM_M2]], writes=[B_gt], tag="g_eq2")
                P.op("vector", lambda v: v.tensor_scalar(
                    gt_sb[:, 24:32], gt_sb[:, 0:8], stat(SM_P1), None,
                    op0=ALU.mult),
                    reads=[B_gt, B_stats[SM_P1]], writes=[B_gt], tag="g_t1")
                P.op("vector", (lambda g8=g8: (lambda v: v.scalar_tensor_tensor(
                    g8, gt_sb[:, 16:24], stat(SM_P2), gt_sb[:, 24:32],
                    op0=ALU.mult, op1=ALU.add)))(),
                    reads=[B_gt, B_stats[SM_P2]], writes=[B_wg], tag="g_w")

            if b == 0 and tc == 0:
                emit_sx(1)
            if b == 0 and tc == 1 and stage == "full" and "nopre" not in opts:
                P.rebind(B_P + B_PT, [B_wq])
                P.dma("gpsimd", lambda g: g.dma_start(
                    out=Wq_sb[:, :].rearrange("p (dc o) -> p dc o", o=D),
                    in_=wq_in.rearrange("(dc p) o -> p dc o", p=128)),
                    writes=[B_wq], tag="wq_pre", klass="ld")
                wq_pre[0] = True

            def emit_ln2(tau):
                par = tau % 2
                tgl = tb + tau
                def _c2(tau=tau, tgl=tgl, par=par):
                    return lambda v: v.scalar_tensor_tensor(
                        tmp_sb2[par][:, :], x_tm[:, tgl * 1024:(tgl + 1) * 1024],
                        1.0, acc[:, tau * 1024:(tau + 1) * 1024],
                        op0=ALU.mult, op1=ALU.add, accum_out=stat(SM_L1S + par))
                P.op("vector", _c2(), reads=[B_xtm[tgl], B_acc[tau]],
                     writes=[B_tmp[par], B_stats[SM_L1S + par]], tag=f"c2_{tc}_{tau}")
                P.op("vector", (lambda par=par: (lambda v: v.tensor_scalar(
                    stat(SM_NMEAN + par), stat(SM_L1S + par), -1.0 / D, None,
                    op0=ALU.mult)))(),
                    reads=[B_stats[SM_L1S + par]], writes=[B_stats[SM_NMEAN + par]])
                P.op("scalar", (lambda par=par: (lambda s: s.activation(
                    sq_sb[:, :], tmp_sb2[par][:, :], AF.Square,
                    bias=stat(SM_NMEAN + par), accum_out=stat(SM_VAR + par))))(),
                    reads=[B_tmp[par], B_stats[SM_NMEAN + par]],
                    writes=[B_stats[SM_VAR + par]])
                P.op("scalar", (lambda par=par: (lambda s: s.activation(
                    stat(SM_RSTD + par), stat(SM_VAR + par), AF.Ln,
                    bias=stat(SM_EPS), scale=1.0 / D)))(),
                    reads=[B_stats[SM_VAR + par], B_stats[SM_EPS]],
                    writes=[B_stats[SM_RSTD + par]])
                P.op("scalar", (lambda par=par: (lambda s: s.activation(
                    stat(SM_RSTD + par), stat(SM_RSTD + par), AF.Exp,
                    scale=-0.5)))(),
                    reads=[B_stats[SM_RSTD + par]], writes=[B_stats[SM_RSTD + par]])
                def _o2(tau=tau, par=par):
                    return lambda v: v.tensor_scalar(
                        acc[:, tau * 1024:(tau + 1) * 1024], tmp_sb2[par][:, :],
                        stat(SM_NMEAN + par), stat(SM_RSTD + par),
                        op0=ALU.add, op1=ALU.mult)
                P.op("vector", _o2(), reads=[B_tmp[par], B_stats[SM_NMEAN + par],
                                             B_stats[SM_RSTD + par]],
                     writes=[B_acc[tau]], tag=f"o2_{tc}_{tau}")


            e_order = list(range(NE)) if tc % 2 == 0 else list(range(NE - 1, -1, -1))
            for e_ in e_order:
                first_e = e_ == e_order[0]
                if b == 0 and tc == 0 and e_ + 2 < NE:
                    emit_chain(e_ + 2)
                if wout_resident[0] != e_:
                    for ic in range(IC):
                        def _wo(e=e_, ic=ic):
                            return lambda s: s.dma_start_transpose(
                                woutT[:, ic * 1024:(ic + 1) * 1024], s_wout[e, ic])
                        P.dma("scalar", _wo(), reads=[B_swout[e_][ic]],
                              writes=[B_woutT[ic]], tag=f"woL{e_}", transpose=True)
                    wout_resident[0] = e_
                for oc2 in range(OC2):
                    sl = oc2 % 2
                    def _wl(e=e_, o=oc2, sl=sl):
                        return lambda s: s.dma_start(
                            out=winT[sl][:, :].rearrange("p (dc o) -> p dc o", o=256),
                            in_=s_winT[e, :, :, o * 256:(o + 1) * 256]
                                .rearrange("dc p o -> p dc o"))
                    P.dma("sync", _wl(), reads=[B_swinT[e_]],
                          writes=[B_winT[sl]], tag=f"wiL{e_}_{oc2}")
                    for sub in range(2):
                        oc = oc2 * 2 + sub
                        for tg in range(2):
                            bank01 = (oc * 2 + tg) % 2
                            bank = psS[:, bank01 * 512:(bank01 + 1) * 512]
                            bbuf = PS[bank01]
                            fns = []
                            for dc in range(DC):
                                def _mm(dc=dc, sub=sub, tg=tg, sl=sl, tc=tc,
                                        bank=bank):
                                    return lambda te: te.matmul(
                                        bank,
                                        winT[sl][:, dc * 256 + sub * 128:
                                                 dc * 256 + (sub + 1) * 128],
                                        XT[:, dc * 2048 + tc * 1024 + tg * 512:
                                           dc * 2048 + tc * 1024 + (tg + 1) * 512],
                                        start=(dc == 0), stop=(dc == DC - 1))
                                fns.append(_mm())
                            P.op("tensor", fns, reads=[B_winT[sl]] + B_X1T,
                                 writes=[bbuf], tag=f"uv{e_}_{oc}_{tg}")
                            if oc < IC:
                                def _ev(oc=oc, tg=tg, bank=bank):
                                    return lambda v: v.tensor_copy(
                                        u_buf[:, oc * 1024 + tg * 512:
                                              oc * 1024 + (tg + 1) * 512], bank)
                                P.op("vector", _ev(), reads=[bbuf],
                                     writes=[B_u[oc * 2 + tg]], tag=f"uev{e_}_{oc}")
                            else:
                                sv = (oc * 2 + tg) % 2
                                def _si(sv=sv, bank=bank):
                                    return lambda s: s.activation(
                                        sv_sb[sv][:, :], bank, AF.Silu)
                                P.op("scalar", _si(), reads=[bbuf],
                                     writes=[B_sv[sv]], tag=f"silu{e_}_{oc}")
                                uc = oc - IC
                                def _h(uc=uc, tg=tg, sv=sv):
                                    return lambda v: v.tensor_mul(
                                        u_buf[:, uc * 1024 + tg * 512:
                                              uc * 1024 + (tg + 1) * 512],
                                        u_buf[:, uc * 1024 + tg * 512:
                                              uc * 1024 + (tg + 1) * 512],
                                        sv_sb[sv][:, :])
                                P.op("vector", _h(), reads=[B_sv[sv],
                                                            B_u[uc * 2 + tg]],
                                     writes=[B_u[uc * 2 + tg]], tag=f"h{e_}_{uc}")
                if first_e:
                    for tau2 in range(TAU):
                        emit_gating(tau2)
                if (b == 0 and tc == NTC - 1 and e_ == e_order[-1] and stage == "full" and "nopre" not in opts):
                    P.rebind(B_X1T, B_XT)
                    for dc in range(DC):
                        def _xt1(dc=dc):
                            return lambda sy: sy.dma_start_transpose(
                                XT[:, dc * 2048:(dc + 1) * 2048], s_x[1, dc])
                        P.dma("sync", _xt1(), reads=[B_sx[1][dc]],
                              writes=[B_XT[dc]], tag="XTpre", transpose=True)
                    xt_pre[0] = True
                for tau in range(TAU):
                    pr = tau % 2
                    banks = (psS[:, 1024:1536], psS[:, 1536:2048]) if pr == 0 \
                        else (psP4[:, :], psP5[:, :])
                    bbufs = [PS[2], PS[3]] if pr == 0 else [PS[4], PS[5]]
                    tg = tau // 4
                    fns = []
                    for ic in range(IC):
                        for half in range(2):
                            def _mm(ic=ic, half=half, tau=tau, banks=banks):
                                return lambda te: te.matmul(
                                    banks[half],
                                    u_buf[:, ic * 1024 + tau * 128:
                                          ic * 1024 + (tau + 1) * 128],
                                    woutT[:, ic * 1024 + half * 512:
                                          ic * 1024 + (half + 1) * 512],
                                    start=(ic == 0), stop=(ic == IC - 1))
                            fns.append(_mm())
                    P.op("tensor", fns,
                         reads=[B_u[ic2 * 2 + tg] for ic2 in range(IC)] + list(B_woutT),
                         writes=bbufs, tag=f"y{e_}_{tau}")
                    for half in range(2):
                        if first_e:
                            def _ac(tau=tau, half=half, e=e_, banks=banks):
                                return lambda v: v.tensor_scalar(
                                    acc[:, tau * 1024 + half * 512:
                                        tau * 1024 + (half + 1) * 512],
                                    banks[half],
                                    wg_sb[:, tau * 8 + e: tau * 8 + e + 1], None,
                                    op0=ALU.mult)
                        else:
                            def _ac(tau=tau, half=half, e=e_, banks=banks):
                                return lambda v: v.scalar_tensor_tensor(
                                    acc[:, tau * 1024 + half * 512:
                                        tau * 1024 + (half + 1) * 512],
                                    banks[half],
                                    wg_sb[:, tau * 8 + e: tau * 8 + e + 1],
                                    acc[:, tau * 1024 + half * 512:
                                        tau * 1024 + (half + 1) * 512],
                                    op0=ALU.mult, op1=ALU.add)
                        P.op("vector", _ac(), reads=[bbufs[half], B_wg, B_acc[tau]],
                             writes=[B_acc[tau]], tag=f"yac{e_}_{tau}_{half}")
                    if e_ == e_order[-1] and "noln2y" not in opts:
                        emit_ln2(tau)

            if "noln2y" in opts:
                for tau in range(TAU):
                    emit_ln2(tau)
            if b == 0:
                emit_xtm(1, tc)
            for h in range(2):
                def _st(b=b, tc=tc, h=h):
                    return lambda g: g.dma_start(
                        out=x_out[b, tc * 1024 + h * 512: tc * 1024 + (h + 1) * 512, :]
                            .rearrange("(t p) d -> p t d", p=128),
                        in_=acc[:, :].rearrange("p (t d) -> p t d", d=D)
                        [:, h * 4:(h + 1) * 4, :])
                P.dma("gpsimd", _st(), reads=B_acc[h * 4:(h + 1) * 4],
                      tag=f"out{b}_{tc}", klass="out")

    if "wdump" in opts:
        dbg = nc.dram_tensor("dbg", [NE, DC, 128, 2 * ISZ], BF16,
                             kind="ExternalOutput").ap()
        dbg0 = nc.dram_tensor("dbg0", [NE, 2 * ISZ, D], BF16,
                              kind="ExternalOutput").ap()
        for e in range(NE):
            P.dma("gpsimd", (lambda e=e: (lambda g: g.dma_start(
                out=dbg[e], in_=s_winT[e])))(),
                reads=[B_swinT[e]], tag="wdump", klass="out")
            P.dma("gpsimd", (lambda e=e: (lambda g: g.dma_start(
                out=dbg0[e], in_=s_win0[e])))(),
                reads=[B_swin0[e]], tag="wdump0", klass="out")
    for (q, klass), (sems, cnt) in P.QS.items():
        if q == "gpsimd" and cnt:
            for j in range(P.RROT):
                nj = (cnt - j + P.RROT - 1) // P.RROT
                if nj > 0:
                    P.raw_wait("gpsimd", sems[j], 16 * nj)

    with st:
        block = st.enter_context(nc.Block())
        P.emit(block)
    return nc


_NC_CACHE = {}


def _get_nc(stage, opts=frozenset()):
    key = (stage, tuple(sorted(opts)))
    if key not in _NC_CACHE:
        _NC_CACHE[key] = build(stage, opts)
    return _NC_CACHE[key]


def _consts():
    ident = np.eye(128, dtype=_BF)
    masks = np.zeros((4, 128, 512), dtype=np.float32)
    for r in range(4):
        for i in range(128):
            masks[r, i, 128 * r + i + 1:] = NEG
    return ident, masks.astype(_BF)


def kernel(L0_attn_x, L0_attn_Wq, L0_attn_Wk, L0_attn_Wv,
           L0_ffn_gate_w, L0_ffn_w_in, L0_ffn_w_out, stage="full",
           opts=frozenset()):
    x = np.ascontiguousarray(np.asarray(L0_attn_x, dtype=np.float32))
    wq = np.ascontiguousarray(np.asarray(L0_attn_Wq, dtype=np.float32))
    wk = np.ascontiguousarray(np.asarray(L0_attn_Wk, dtype=np.float32))
    wv = np.ascontiguousarray(np.asarray(L0_attn_Wv, dtype=np.float32))
    gw = np.ascontiguousarray(np.asarray(L0_ffn_gate_w, dtype=np.float32))
    win = np.ascontiguousarray(np.asarray(L0_ffn_w_in, dtype=np.float32))
    wout = np.ascontiguousarray(np.asarray(L0_ffn_w_out, dtype=np.float32))
    ident, masks = _consts()
    nc = _get_nc(stage, opts)
    in_maps = []
    for c in range(NCORES):
        in_maps.append({
            "x": x[NB * c: NB * (c + 1)],
            "wq": wq, "wk": wk, "wv": wv, "gw": gw,
            "win": win, "wout": wout,
            "ident": ident, "masks": masks,
        })
    res = run_bass_kernel_spmd(nc, in_maps, core_ids=list(range(NCORES)))
    out = np.concatenate([np.asarray(r["x_out"]) for r in res.results], axis=0)
    return out.astype(np.float32, copy=False)
